# version 01 (final)
# baseline (speedup 1.0000x reference)
import numpy as np
from contextlib import ExitStack

import concourse.bass as bass
import concourse.mybir as mybir
from concourse.bass_utils import run_bass_kernel_spmd

F32 = mybir.dt.float32
BF16 = mybir.dt.bfloat16
AF = mybir.ActivationFunctionType
ALU = mybir.AluOpType
AX = mybir.AxisListType

NC = 8
B0 = 2
T0 = 8192
D0 = 1024
M0 = B0 * T0
TPC = M0 // NC

_IDENT = np.eye(128, dtype=np.float32)
_TRI = np.triu(np.ones((128, 128), dtype=np.float32))

_prog_cache = {}


class Seq:

    def __init__(self, nc):
        self.nc = nc
        self.steps = []

    def step(self, engine, fn, inc=1):
        self.steps.append((engine, fn, inc))

    def dma(self, q, out, in_):
        self.step(q, lambda e: e.dma_start(out=out, in_=in_), inc=16)

    def build(self):
        nc = self.nc
        engines_used = sorted({s[0] for s in self.steps})
        with nc.semaphore("tick") as tick, nc.Block() as block:
            total = sum(s[2] for s in self.steps)
            for en in engines_used:
                def mk(en):
                    def emit(eng):
                        if en == "tensor":
                            eng = _TEProxy(eng)
                        before = 0
                        for (sen, fn, inc) in self.steps:
                            if sen == en:
                                if before > 0:
                                    eng.wait_ge(tick, before)
                                inst = fn(eng)
                                inst.then_inc(tick, inc)
                            before += inc
                        eng.wait_ge(tick, total)
                    return emit
                getattr(block, en)(mk(en))


class _TEProxy:
    def __init__(self, eng):
        self._e = eng

    def matmul(self, *a, **k):
        k.setdefault("skip_group_check", True)
        return self._e.matmul(*a, **k)

    def __getattr__(self, n):
        return getattr(self._e, n)


class Bld:
    def __init__(self):
        self.nc = bass.Bass("TRN2", target_bir_lowering=False, debug=False)
        self.sq = Seq(self.nc)
        self.ctx = ExitStack()
        self._consts = {}

    def sbuf(self, name, shape, dt=F32):
        return self.ctx.enter_context(self.nc.sbuf_tensor(name, shape, dt))

    def psum(self, name, shape, dt=F32):
        return self.ctx.enter_context(self.nc.psum_tensor(name, shape, dt))

    def inp(self, name, shape):
        return self.nc.dram_tensor(name, list(shape), F32, kind="ExternalInput").ap()

    def outp(self, name, shape):
        return self.nc.dram_tensor(name, list(shape), F32, kind="ExternalOutput").ap()

    def const(self, name, arr):
        if name not in self._consts:
            self._consts[name] = self.nc.inline_tensor(
                np.ascontiguousarray(arr.astype(np.float32)), name=name).ap()
        return self._consts[name]

    def finish(self):
        self.sq.build()
        self.ctx.close()
        return self.nc

    def mk_common(self, Dmax=1024):
        self.ident_f = self.sbuf("identf", [128, 128], F32)
        self.ident = self.sbuf("identb", [128, 128], BF16)
        self.ptp = self.psum("ptp", [128, 128], BF16)
        self.prep = self.psum("prep", [128, 512], F32)
        self.ones_f = self.sbuf("onesf", [1, 128], F32)
        self.ones_col_b = self.sbuf("onescolb", [128, 1], BF16)
        ic = self.const("identc", _IDENT)
        self.sq.dma("sync", self.ident_f[:, :], ic)
        self.sq.step("vector", lambda e: e.tensor_copy(out=self.ident[:, :], in_=self.ident_f[:, :]))
        self.sq.step("vector", lambda e: e.memset(self.ones_f[:, :], 1.0))
        self.sq.step("vector", lambda e: e.memset(self.ones_col_b[:, :], 1.0))
        self.ln_m = self.sbuf("ln_m", [128, 1], F32)
        self.ln_m2 = self.sbuf("ln_m2", [128, 1], F32)
        self.ln_xc = self.sbuf("ln_xc", [128, Dmax], F32)
        self.ln_sq = self.sbuf("ln_sq", [128, Dmax], F32)
        self.ln_v = self.sbuf("ln_v", [128, 1], F32)
        self.ln_sd = self.sbuf("ln_sd", [128, 1], F32)
        self.ln_r = self.sbuf("ln_r", [128, 1], F32)

    def ln(self, xs, D, out_bf=None, out_f32=None, g_rep=None, b_rep=None, eps=1e-5):
        sq, s = self.sq, self
        sq.step("vector", lambda e: e.tensor_reduce(out=s.ln_m[:, :], in_=xs, axis=AX.X, op=ALU.add))
        sq.step("vector", lambda e: e.tensor_scalar_mul(out=s.ln_m2[:, :], in0=s.ln_m[:, :], scalar1=-1.0 / D))
        sq.step("vector", lambda e: e.tensor_scalar_add(out=s.ln_xc[:, 0:D], in0=xs, scalar1=s.ln_m2[:, 0:1]))
        sq.step("scalar", lambda e: e.activation(out=s.ln_sq[:, 0:D], in_=s.ln_xc[:, 0:D], func=AF.Square))
        sq.step("vector", lambda e: e.tensor_reduce(out=s.ln_v[:, :], in_=s.ln_sq[:, 0:D], axis=AX.X, op=ALU.add))
        sq.step("vector", lambda e: e.tensor_scalar(out=s.ln_v[:, :], in0=s.ln_v[:, :], scalar1=1.0 / D,
                                                    scalar2=eps, op0=ALU.mult, op1=ALU.add))
        sq.step("scalar", lambda e: e.activation(out=s.ln_sd[:, :], in_=s.ln_v[:, :], func=AF.Sqrt))
        sq.step("vector", lambda e: e.reciprocal(out=s.ln_r[:, :], in_=s.ln_sd[:, :]))
        if g_rep is None:
            if out_f32 is not None:
                sq.step("vector", lambda e: e.tensor_scalar_mul(out=out_f32, in0=s.ln_xc[:, 0:D], scalar1=s.ln_r[:, 0:1]))
                if out_bf is not None:
                    sq.step("vector", lambda e: e.tensor_copy(out=out_bf, in_=out_f32))
            else:
                sq.step("vector", lambda e: e.tensor_scalar_mul(out=out_bf, in0=s.ln_xc[:, 0:D], scalar1=s.ln_r[:, 0:1]))
        else:
            sq.step("vector", lambda e: e.tensor_scalar_mul(out=s.ln_xc[:, 0:D], in0=s.ln_xc[:, 0:D], scalar1=s.ln_r[:, 0:1]))
            tgt = out_f32 if out_f32 is not None else out_bf
            sq.step("vector", lambda e: e.tensor_mul(out=s.ln_xc[:, 0:D], in0=s.ln_xc[:, 0:D], in1=g_rep))
            sq.step("vector", lambda e: e.tensor_add(out=tgt, in0=s.ln_xc[:, 0:D], in1=b_rep))
            if out_f32 is not None and out_bf is not None:
                sq.step("vector", lambda e: e.tensor_copy(out=out_bf, in_=out_f32))

    def tp(self, src_bf, dst_bf, P=128, F=128):
        sq, s = self.sq, self
        sq.step("tensor", lambda e: e.transpose(s.ptp[0:F, 0:P], in_=src_bf, identity=s.ident[:, :]))
        sq.step("vector", lambda e: e.tensor_copy(out=dst_bf, in_=s.ptp[0:F, 0:P]))

    def rep_row(self, src_row, dst_sb, W):
        sq, s = self.sq, self
        for w0 in range(0, W, 512):
            w = min(512, W - w0)
            sq.step("tensor", lambda e, w0=w0, w=w: e.matmul(
                s.prep[0:128, 0:w], lhsT=s.ones_f[0:1, :], rhs=src_row[0:1, w0:w0 + w], start=True, stop=True))
            sq.step("vector", lambda e, w0=w0, w=w: e.tensor_copy(out=dst_sb[:, w0:w0 + w], in_=s.prep[0:128, 0:w]))



def build_lin_attn():
    b = Bld()
    x = b.inp("x_sl", (T0, D0))
    wq = b.inp("wqh", (D0, 256))
    wk = b.inp("wkh", (D0, 256))
    wv = b.inp("wvh", (D0, 256))
    bq = b.inp("bqh", (256,))
    bk = b.inp("bkh", (256,))
    bv = b.inp("bvh", (256,))
    osh = b.outp("o_sh", (T0, 256))

    b.mk_common(D0)
    tri = b.sbuf("tri", [128, 128], F32)
    b.sq.dma("sync", tri[:, :], b.const("tric", _TRI))

    xs = b.sbuf("xs", [128, D0], F32)
    xn = b.sbuf("xn", [128, D0], BF16)
    xnT = b.sbuf("xnT", [128, 8, 128], BF16)
    wsb = {}
    for nm, w in (("q", wq), ("k", wk), ("v", wv)):
        f = b.sbuf(f"w{nm}f", [128, 8, 256], F32)
        bb = b.sbuf(f"w{nm}b", [128, 8, 256], BF16)
        for kt in range(8):
            b.sq.dma("sync", f[:, kt, :], w[kt * 128:(kt + 1) * 128, :])
        b.sq.step("vector", lambda e, f=f, bb=bb: e.tensor_copy(out=bb[:, :, :], in_=f[:, :, :]))
        wsb[nm] = bb
    bsb = {}
    for nm, bias in (("q", bq), ("k", bk), ("v", bv)):
        t = b.sbuf(f"b{nm}", [64, 4], F32)
        for h in range(4):
            b.sq.dma("sync", t[:, h:h + 1], bias[h * 64:(h + 1) * 64].unsqueeze(1))
        bsb[nm] = t

    pqkv = b.psum("pqkv", [64, 128], F32)
    qf = b.sbuf("qf", [64, 128], F32)
    fr = b.sbuf("fr", [64, 128], F32)
    fm = b.sbuf("fm", [64, 128], F32)
    fe = b.sbuf("fe", [64, 128], F32)
    qT = b.sbuf("qT", [64, 128], BF16)
    kT = b.sbuf("kT", [64, 128], BF16)
    vT = b.sbuf("vT", [64, 128], BF16)
    kr = b.sbuf("kr", [128, 64], BF16)
    vr = b.sbuf("vr", [128, 64], BF16)
    ptp64 = b.psum("ptp64", [128, 64], BF16)
    S_all = b.psum("S_all", [64, 4, 65], F32)
    S_sb = b.sbuf("S_sb", [64, 65], BF16)
    pA = b.psum("pA", [128, 128], F32)
    Am = b.sbuf("Am", [128, 128], BF16)
    pN = b.psum("pN", [64, 128], F32)
    pD = b.psum("pD", [128, 1], F32)
    den = b.sbuf("den", [128, 1], F32)
    rcp = b.sbuf("rcp", [128, 1], F32)
    nbf = b.sbuf("nbf", [64, 128], BF16)
    obuf = b.sbuf("obuf", [128, 256], F32)
    sq = b.sq

    def featmap(src_ps, bias_col, dst_bf):
        sq.step("vector", lambda e: e.tensor_scalar_add(out=qf[:, :], in0=src_ps, scalar1=bias_col))
        sq.step("vector", lambda e: e.tensor_scalar_max(out=fr[:, :], in0=qf[:, :], scalar1=0.0))
        sq.step("vector", lambda e: e.tensor_scalar_min(out=fm[:, :], in0=qf[:, :], scalar1=0.0))
        sq.step("scalar", lambda e: e.activation(out=fe[:, :], in_=fm[:, :], func=AF.Exp))
        sq.step("vector", lambda e: e.tensor_add(out=dst_bf, in0=fr[:, :], in1=fe[:, :]))

    for c in range(64):
        r0 = c * 128
        sq.dma("sync", xs[:, :], x[r0:r0 + 128, :])
        b.ln(xs[:, :], D0, out_bf=xn[:, :])
        for kt in range(8):
            b.tp(xn[:, kt * 128:(kt + 1) * 128], xnT[:, kt, :])
        for h in range(4):
            for nm, dst in (("q", qT), ("k", kT)):
                for kt in range(8):
                    sq.step("tensor", lambda e, nm=nm, kt=kt, h=h: e.matmul(
                        pqkv[:, :], lhsT=wsb[nm][:, kt, h * 64:(h + 1) * 64], rhs=xnT[:, kt, :],
                        start=(kt == 0), stop=(kt == 7)))
                featmap(pqkv[:, :], bsb[nm][:, h:h + 1], dst[:, :])
            for kt in range(8):
                sq.step("tensor", lambda e, kt=kt, h=h: e.matmul(
                    pqkv[:, :], lhsT=wsb["v"][:, kt, h * 64:(h + 1) * 64], rhs=xnT[:, kt, :],
                    start=(kt == 0), stop=(kt == 7)))
            sq.step("vector", lambda e, h=h: e.tensor_scalar_add(out=vT[:, :], in0=pqkv[:, :], scalar1=bsb["v"][:, h:h + 1]))
            sq.step("tensor", lambda e: e.transpose(ptp64[:, :], in_=vT[:, :], identity=b.ident[0:64, 0:64]))
            sq.step("vector", lambda e: e.tensor_copy(out=vr[:, :], in_=ptp64[:, :]))
            sq.step("tensor", lambda e: e.transpose(ptp64[:, :], in_=kT[:, :], identity=b.ident[0:64, 0:64]))
            sq.step("vector", lambda e: e.tensor_copy(out=kr[:, :], in_=ptp64[:, :]))
            if c > 0:
                sq.step("vector", lambda e, h=h: e.tensor_copy(out=S_sb[:, :], in_=S_all[:, h, :]))
            sq.step("tensor", lambda e: e.matmul(pA[:, :], lhsT=kT[:, :], rhs=qT[:, :], start=True, stop=True))
            sq.step("vector", lambda e: e.tensor_tensor(out=Am[:, :], in0=pA[:, :], in1=tri[:, :], op=ALU.mult))
            sq.step("tensor", lambda e, c=c: e.matmul(pN[:, :], lhsT=vr[:, :], rhs=Am[:, :], start=True, stop=(c == 0)))
            if c > 0:
                sq.step("tensor", lambda e: e.matmul(pN[:, :], lhsT=S_sb[:, 0:64], rhs=qT[:, :], start=False, stop=True))
            sq.step("tensor", lambda e, c=c: e.matmul(pD[:, :], lhsT=Am[:, :], rhs=b.ones_col_b[:, :], start=True, stop=(c == 0)))
            if c > 0:
                sq.step("tensor", lambda e: e.matmul(pD[:, :], lhsT=qT[:, :], rhs=S_sb[:, 64:65], start=False, stop=True))
            sq.step("vector", lambda e: e.tensor_scalar_max(out=den[:, :], in0=pD[:, :], scalar1=1e-6))
            sq.step("vector", lambda e: e.reciprocal(out=rcp[:, :], in_=den[:, :]))
            sq.step("vector", lambda e: e.tensor_copy(out=nbf[:, :], in_=pN[:, :]))
            sq.step("tensor", lambda e: e.transpose(ptp64[:, :], in_=nbf[:, :], identity=b.ident[0:64, 0:64]))
            sq.step("vector", lambda e, h=h: e.tensor_scalar_mul(out=obuf[:, h * 64:(h + 1) * 64], in0=ptp64[:, :],
                                                                 scalar1=rcp[:, 0:1]))
            sq.step("tensor", lambda e, c=c, h=h: e.matmul(S_all[:, h, 0:64], lhsT=kr[:, :], rhs=vr[:, :],
                                                           start=(c == 0), stop=(c == 63)))
            sq.step("tensor", lambda e, c=c, h=h: e.matmul(S_all[:, h, 64:65], lhsT=kr[:, :], rhs=b.ones_col_b[:, :],
                                                           start=(c == 0), stop=(c == 63)))
        sq.dma("gpsimd", osh[r0:r0 + 128, :], obuf[:, :])
    return b.finish()



def build_proj_res(KD, ND, residual, ln_input, add_ln, has_bias, rows=TPC):
    b = Bld()
    a_in = b.inp("a_in", (rows, KD))
    xr = b.inp("x_in", (rows, ND)) if (residual or add_ln) else None
    wT = b.inp("wT", (KD, ND))
    bias = b.inp("bias", (ND,)) if has_bias else None
    out = b.outp("o_out", (rows, ND))

    b.mk_common(max(KD, ND))
    KT = KD // 128
    wf = b.sbuf("wf", [128, KT, ND], F32)
    wb = b.sbuf("wb", [128, KT, ND], BF16)
    for kt in range(KT):
        b.sq.dma("sync", wf[:, kt, :], wT[kt * 128:(kt + 1) * 128, :])
    b.sq.step("vector", lambda e: e.tensor_copy(out=wb[:, :, :], in_=wf[:, :, :]))
    brep = None
    if has_bias:
        brow = b.sbuf("brow", [1, ND], F32)
        b.sq.dma("sync", brow[:, :], bias.unsqueeze(0))
        brep = b.sbuf("brep", [128, ND], F32)
        b.rep_row(brow, brep, ND)

    asb = b.sbuf("asb", [128, KD], F32)
    abf = b.sbuf("abf", [128, KD], BF16)
    aT = b.sbuf("aT", [128, KT, 128], BF16)
    xsb = b.sbuf("xsb", [128, ND], F32)
    lnf = b.sbuf("lnf", [128, ND], F32)
    osb = b.sbuf("osb", [128, ND], F32)
    pmm = [b.psum(f"pmm{i}", [128, 512], F32) for i in range(ND // 512)]
    sq = b.sq

    for t in range(rows // 128):
        r0 = t * 128
        sq.dma("sync", asb[:, :], a_in[r0:r0 + 128, :])
        if ln_input:
            b.ln(asb[:, :], KD, out_bf=abf[:, :])
        else:
            sq.step("vector", lambda e: e.tensor_copy(out=abf[:, :], in_=asb[:, :]))
        for kt in range(KT):
            b.tp(abf[:, kt * 128:(kt + 1) * 128], aT[:, kt, :])
        for nb in range(ND // 512):
            for kt in range(KT):
                sq.step("tensor", lambda e, nb=nb, kt=kt: e.matmul(
                    pmm[nb][:, :], lhsT=aT[:, kt, :], rhs=wb[:, kt, nb * 512:(nb + 1) * 512],
                    start=(kt == 0), stop=(kt == KT - 1)))
        if xr is not None:
            sq.dma("sync", xsb[:, :], xr[r0:r0 + 128, :])
        for nb in range(ND // 512):
            slc = slice(nb * 512, (nb + 1) * 512)
            if has_bias:
                sq.step("vector", lambda e, nb=nb, slc=slc: e.tensor_tensor(
                    out=osb[:, slc], in0=pmm[nb][:, :], in1=brep[:, slc], op=ALU.add))
            else:
                sq.step("vector", lambda e, nb=nb, slc=slc: e.tensor_copy(out=osb[:, slc], in_=pmm[nb][:, :]))
        if residual:
            sq.step("vector", lambda e: e.tensor_add(out=osb[:, :], in0=osb[:, :], in1=xsb[:, :]))
        if add_ln:
            b.ln(xsb[:, :], ND, out_f32=lnf[:, :])
            sq.step("vector", lambda e: e.tensor_add(out=osb[:, :], in0=osb[:, :], in1=lnf[:, :]))
        sq.dma("gpsimd", out[r0:r0 + 128, :], osb[:, :])
    return b.finish()



def build_ffn_ternary(rows=TPC):
    b = Bld()
    DH = 4096
    x = b.inp("x_in", (rows, D0))
    w1 = b.inp("w1T", (D0, DH))
    w2 = b.inp("w2T", (D0, DH))
    h1 = b.inp("h1T", (DH, DH))
    h2 = b.inp("h2T", (DH, DH))
    w3 = b.inp("w3T", (DH, D0))
    s1 = b.inp("s1", (128, DH // 128))
    s2 = b.inp("s2", (128, DH // 128))
    sh1 = b.inp("sh1", (128, DH // 128))
    sh2 = b.inp("sh2", (128, DH // 128))
    s3 = b.inp("s3", (128, D0 // 128))
    out = b.outp("o_out", (rows, D0))

    b.mk_common(D0)
    sq = b.sq
    s1s = b.sbuf("s1s", [128, DH // 128], F32)
    s2s = b.sbuf("s2s", [128, DH // 128], F32)
    sh1s = b.sbuf("sh1s", [128, DH // 128], F32)
    sh2s = b.sbuf("sh2s", [128, DH // 128], F32)
    s3s = b.sbuf("s3s", [128, D0 // 128], F32)
    for ap_, t_ in ((s1, s1s), (s2, s2s), (sh1, sh1s), (sh2, sh2s), (s3, s3s)):
        sq.dma("sync", t_[:, :], ap_)

    ST = 512
    nst = rows // ST
    xs = b.sbuf("xs", [128, D0], F32)
    xnb = b.sbuf("xnb", [128, D0], BF16)
    xT = b.sbuf("xT", [128, 8, ST], BF16)
    wstg = b.sbuf("wstg", [128, 512], F32)
    wstgb = b.sbuf("wstgb", [128, 512], BF16)
    hT = b.sbuf("hT", [128, 32, ST], BF16)
    gT = b.sbuf("gT", [128, 32, ST], BF16)
    t1 = b.sbuf("t1", [128, ST], F32)
    t2 = b.sbuf("t2", [128, ST], F32)
    oT = b.sbuf("oT", [128, 8, ST], BF16)
    osb = b.sbuf("osb", [128, D0], F32)
    pA = b.psum("pA", [128, ST], F32)
    pB = b.psum("pB", [128, ST], F32)

    def stream_mm(wap, K, nb, psum, rhs_tiles, nkt):
        for kt in range(nkt):
            sq.dma("sync", wstg[:, 0:128], wap[kt * 128:(kt + 1) * 128, nb * 128:(nb + 1) * 128])
            sq.step("vector", lambda e: e.tensor_copy(out=wstgb[:, 0:128], in_=wstg[:, 0:128]))
            sq.step("tensor", lambda e, kt=kt: e.matmul(
                psum[:, :], lhsT=wstgb[:, 0:128], rhs=rhs_tiles[:, kt, :], start=(kt == 0), stop=(kt == nkt - 1)))

    for st in range(nst):
        base = st * ST
        for r in range(ST // 128):
            sq.dma("sync", xs[:, :], x[base + r * 128: base + (r + 1) * 128, :])
            b.ln(xs[:, :], D0, out_bf=xnb[:, :])
            for kt in range(8):
                b.tp(xnb[:, kt * 128:(kt + 1) * 128], xT[:, kt, r * 128:(r + 1) * 128])
        for nb in range(32):
            stream_mm(w1, D0, nb, pA, xT, 8)
            stream_mm(w2, D0, nb, pB, xT, 8)
            sq.step("scalar", lambda e, nb=nb: e.activation(out=t1[:, :], in_=pA[:, :], func=AF.Silu, scale=s1s[:, nb:nb + 1]))
            sq.step("vector", lambda e, nb=nb: e.tensor_scalar_mul(out=t2[:, :], in0=pB[:, :], scalar1=s2s[:, nb:nb + 1]))
            sq.step("vector", lambda e, nb=nb: e.tensor_mul(out=gT[:, nb, :], in0=t1[:, :], in1=t2[:, :]))
        for nb in range(32):
            stream_mm(h1, DH, nb, pA, gT, 32)
            sq.step("scalar", lambda e, nb=nb: e.activation(out=hT[:, nb, :], in_=pA[:, :], func=AF.Gelu, scale=sh1s[:, nb:nb + 1]))
        for nb in range(32):
            stream_mm(h2, DH, nb, pA, hT, 32)
            sq.step("scalar", lambda e, nb=nb: e.activation(out=gT[:, nb, :], in_=pA[:, :], func=AF.Gelu, scale=sh2s[:, nb:nb + 1]))
        for nb in range(8):
            stream_mm(w3, DH, nb, pA, gT, 32)
            sq.step("vector", lambda e, nb=nb: e.tensor_scalar_mul(out=oT[:, nb, :], in0=pA[:, :], scalar1=s3s[:, nb:nb + 1]))
        for r in range(ST // 128):
            sq.dma("sync", xs[:, :], x[base + r * 128: base + (r + 1) * 128, :])
            for nb in range(8):
                sq.step("tensor", lambda e, nb=nb, r=r: e.transpose(
                    b.ptp[:, :], in_=oT[:, nb, r * 128:(r + 1) * 128], identity=b.ident[:, :]))
                sq.step("vector", lambda e, nb=nb: e.tensor_tensor(
                    out=osb[:, nb * 128:(nb + 1) * 128], in0=b.ptp[:, :], in1=xs[:, nb * 128:(nb + 1) * 128], op=ALU.add))
            sq.dma("gpsimd", out[base + r * 128: base + (r + 1) * 128, :], osb[:, :])
    return b.finish()



def build_l1qkv():
    b = Bld()
    x = b.inp("x_in", (TPC, D0))
    wq = b.inp("wqT", (D0, 64))
    wk = b.inp("wkT", (D0, 64))
    wv = b.inp("wvT", (D0, 64))
    bq = b.inp("bq", (64,))
    bk = b.inp("bk", (64,))
    bv = b.inp("bv", (64,))
    qTo = b.outp("qT_sh", (64, TPC))
    kTo = b.outp("kT_sh", (64, TPC))
    vo = b.outp("v_sh", (TPC, 64))

    b.mk_common(D0)
    sq = b.sq
    wsb = {}
    for nm, w in (("q", wq), ("k", wk), ("v", wv)):
        f = b.sbuf(f"w{nm}f", [128, 8, 64], F32)
        bb = b.sbuf(f"w{nm}b", [128, 8, 64], BF16)
        for kt in range(8):
            sq.dma("sync", f[:, kt, :], w[kt * 128:(kt + 1) * 128, :])
        sq.step("vector", lambda e, f=f, bb=bb: e.tensor_copy(out=bb[:, :, :], in_=f[:, :, :]))
        wsb[nm] = bb
    bsb = {}
    for nm, bias in (("q", bq), ("k", bk), ("v", bv)):
        t = b.sbuf(f"bc{nm}", [64, 1], F32)
        sq.dma("sync", t[:, :], bias.unsqueeze(1))
        bsb[nm] = t

    xs = b.sbuf("xs", [128, D0], F32)
    xnb = b.sbuf("xnb", [128, D0], BF16)
    xT = b.sbuf("xT", [128, 8, 128], BF16)
    pq = b.psum("pq", [64, 128], F32)
    qsb = b.sbuf("qsb", [64, 128], F32)
    vbf = b.sbuf("vbf", [64, 128], BF16)
    ptp64 = b.psum("ptp64", [128, 64], BF16)
    vrow = b.sbuf("vrow", [128, 64], F32)

    for t in range(TPC // 128):
        r0 = t * 128
        sq.dma("sync", xs[:, :], x[r0:r0 + 128, :])
        b.ln(xs[:, :], D0, out_bf=xnb[:, :])
        for kt in range(8):
            b.tp(xnb[:, kt * 128:(kt + 1) * 128], xT[:, kt, :])
        for nm, dst in (("q", qTo), ("k", kTo)):
            for kt in range(8):
                sq.step("tensor", lambda e, nm=nm, kt=kt: e.matmul(
                    pq[:, :], lhsT=wsb[nm][:, kt, :], rhs=xT[:, kt, :], start=(kt == 0), stop=(kt == 7)))
            sq.step("vector", lambda e, nm=nm: e.tensor_scalar_add(out=qsb[:, :], in0=pq[:, :], scalar1=bsb[nm][:, 0:1]))
            sq.dma("gpsimd", dst[:, r0:r0 + 128], qsb[:, :])
        for kt in range(8):
            sq.step("tensor", lambda e, kt=kt: e.matmul(
                pq[:, :], lhsT=wsb["v"][:, kt, :], rhs=xT[:, kt, :], start=(kt == 0), stop=(kt == 7)))
        sq.step("vector", lambda e: e.tensor_scalar_add(out=vbf[:, :], in0=pq[:, :], scalar1=bsb["v"][:, 0:1]))
        sq.step("tensor", lambda e: e.transpose(ptp64[:, :], in_=vbf[:, :], identity=b.ident[0:64, 0:64]))
        sq.step("vector", lambda e: e.tensor_copy(out=vrow[:, :], in_=ptp64[:, :]))
        sq.dma("gpsimd", vo[r0:r0 + 128, :], vrow[:, :])
    return b.finish()



def build_l1flash():
    b = Bld()
    qT = b.inp("qT", (64, TPC))
    kT = b.inp("kT_full", (64, M0))
    vf = b.inp("v_full", (M0, 64))
    x = b.inp("x_in", (TPC, D0))
    out = b.outp("o_out", (TPC, D0))

    b.mk_common(D0)
    sq = b.sq
    kTf = b.sbuf("kTf", [64, M0], F32)
    kTb = b.sbuf("kTb", [64, M0], BF16)
    sq.dma("sync", kTf[:, 0:M0 // 2], kT[:, 0:M0 // 2])
    sq.dma("sync", kTf[:, M0 // 2:], kT[:, M0 // 2:])
    sq.step("vector", lambda e: e.tensor_copy(out=kTb[:, :], in_=kTf[:, :]))
    vsf = b.sbuf("vsf", [128, 128, 64], F32)
    vsb = b.sbuf("vsb", [128, 128, 64], BF16)
    for a in range(128):
        sq.dma("sync", vsf[:, a, :], vf[a * 128:(a + 1) * 128, :])
    sq.step("vector", lambda e: e.tensor_copy(out=vsb[:, :, :], in_=vsf[:, :, :]))
    qTf = b.sbuf("qTf", [64, TPC], F32)
    qTb = b.sbuf("qTb", [64, TPC], BF16)
    sq.dma("sync", qTf[:, :], qT)
    sq.step("vector", lambda e: e.tensor_copy(out=qTb[:, :], in_=qTf[:, :]))

    pst = b.psum("pst", [128, 128], F32)
    Pbf = b.sbuf("Pbf", [128, 128], BF16)
    pyT = b.psum("pyT", [64, 128], F32)
    pden = b.psum("pden", [128, 1], F32)
    rcp = b.sbuf("rcp", [128, 1], F32)
    ybf = b.sbuf("ybf", [64, 128], BF16)
    ptp64 = b.psum("ptp64", [128, 64], BF16)
    ysb = b.sbuf("ysb", [128, 64], F32)
    xsb = b.sbuf("xsb", [128, D0], F32)
    osb = b.sbuf("osb", [128, D0], F32)
    SCALE = 1.0 / 32.0

    for t in range(TPC // 128):
        r0 = t * 128
        for jb in range(128):
            sq.step("tensor", lambda e, jb=jb, r0=r0: e.matmul(
                pst[:, :], lhsT=kTb[:, jb * 128:(jb + 1) * 128], rhs=qTb[:, r0:r0 + 128], start=True, stop=True))
            sq.step("scalar", lambda e: e.activation(out=Pbf[:, :], in_=pst[:, :], func=AF.Exp, scale=SCALE))
            sq.step("tensor", lambda e, jb=jb: e.matmul(
                pyT[:, :], lhsT=vsb[:, jb, :], rhs=Pbf[:, :], start=(jb == 0), stop=(jb == 127)))
            sq.step("tensor", lambda e, jb=jb: e.matmul(
                pden[:, :], lhsT=Pbf[:, :], rhs=b.ones_col_b[:, :], start=(jb == 0), stop=(jb == 127)))
        sq.step("vector", lambda e: e.reciprocal(out=rcp[:, :], in_=pden[:, :]))
        sq.step("vector", lambda e: e.tensor_copy(out=ybf[:, :], in_=pyT[:, :]))
        sq.step("tensor", lambda e: e.transpose(ptp64[:, :], in_=ybf[:, :], identity=b.ident[0:64, 0:64]))
        sq.step("vector", lambda e: e.tensor_scalar_mul(out=ysb[:, :], in0=ptp64[:, :], scalar1=rcp[:, 0:1]))
        sq.dma("sync", xsb[:, :], x[r0:r0 + 128, :])
        for j in range(16):
            sq.step("vector", lambda e, j=j: e.tensor_tensor(
                out=osb[:, j * 64:(j + 1) * 64], in0=xsb[:, j * 64:(j + 1) * 64], in1=ysb[:, :], op=ALU.add))
        sq.dma("gpsimd", out[r0:r0 + 128, :], osb[:, :])
    return b.finish()



def emit_fp4(b, x_f32, out_bf, W, bufs):
    sq = b.sq
    G = W // 16
    s, r, rf, xn, a, q, tq, sg = bufs
    xv = x_f32.rearrange("p (g k) -> p g k", k=16)
    sq.step("vector", lambda e: e.tensor_reduce(out=s[:, 0:G], in_=xv, axis=AX.X, op=ALU.max, apply_absolute_value=True))
    sq.step("vector", lambda e: e.tensor_scalar(out=s[:, 0:G], in0=s[:, 0:G], scalar1=1.0 / 6.0, scalar2=1e-12,
                                                op0=ALU.mult, op1=ALU.max))
    sq.step("vector", lambda e: e.reciprocal(out=r[:, 0:G], in_=s[:, 0:G]))
    for j in range(16):
        sq.step("vector", lambda e, j=j: e.tensor_copy(out=rf[:, 0:W][:, j::16], in_=r[:, 0:G]))
    sq.step("vector", lambda e: e.tensor_mul(out=xn[:, 0:W], in0=x_f32, in1=rf[:, 0:W]))
    sq.step("scalar", lambda e: e.activation(out=a[:, 0:W], in_=xn[:, 0:W], func=AF.Abs))
    sq.step("vector", lambda e: e.tensor_scalar(out=q[:, 0:W], in0=a[:, 0:W], scalar1=0.25, scalar2=0.5,
                                                op0=ALU.is_gt, op1=ALU.mult))
    for thr, wt in ((0.75, 0.5), (1.25, 0.5), (1.75, 0.5), (2.5, 1.0), (3.5, 1.0), (5.0, 2.0)):
        sq.step("vector", lambda e, thr=thr, wt=wt: e.tensor_scalar(out=tq[:, 0:W], in0=a[:, 0:W], scalar1=thr,
                                                                    scalar2=wt, op0=ALU.is_gt, op1=ALU.mult))
        sq.step("vector", lambda e: e.tensor_add(out=q[:, 0:W], in0=q[:, 0:W], in1=tq[:, 0:W]))
    sq.step("scalar", lambda e: e.sign(out=sg[:, 0:W], in_=xn[:, 0:W]))
    sq.step("vector", lambda e: e.tensor_mul(out=q[:, 0:W], in0=q[:, 0:W], in1=sg[:, 0:W]))
    for j in range(16):
        sq.step("vector", lambda e, j=j: e.tensor_copy(out=rf[:, 0:W][:, j::16], in_=s[:, 0:G]))
    sq.step("vector", lambda e: e.tensor_mul(out=out_bf, in0=q[:, 0:W], in1=rf[:, 0:W]))


def mk_fp4_bufs(b, W):
    G = W // 16
    tq = b.sbuf("fq_tq", [128, W], F32)
    return (b.sbuf("fq_s", [128, G], F32), b.sbuf("fq_r", [128, G], F32),
            b.sbuf("fq_rf", [128, W], F32), b.sbuf("fq_xn", [128, W], F32),
            b.sbuf("fq_a", [128, W], F32), b.sbuf("fq_q", [128, W], F32),
            tq, tq)


def build_l1wq(ntiles, W):
    b = Bld()
    win = b.inp("w_in", (ntiles * 128, W))
    wout = b.outp("wqT_out", (W, ntiles * 128))
    b.mk_common(W)
    sq = b.sq
    bufs = mk_fp4_bufs(b, W)
    xs = b.sbuf("xs", [128, W], F32)
    qb = b.sbuf("qb", [128, W], BF16)
    of = b.sbuf("of", [128, 128], F32)
    for t in range(ntiles):
        sq.dma("sync", xs[:, :], win[t * 128:(t + 1) * 128, :])
        emit_fp4(b, xs[:, :], qb[:, :], W, bufs)
        for kb in range(W // 128):
            sq.step("tensor", lambda e, kb=kb: e.transpose(b.ptp[:, :], in_=qb[:, kb * 128:(kb + 1) * 128],
                                                           identity=b.ident[:, :]))
            sq.step("vector", lambda e: e.tensor_copy(out=of[:, :], in_=b.ptp[:, :]))
            sq.dma("gpsimd", wout[kb * 128:(kb + 1) * 128, t * 128:(t + 1) * 128], of[:, :])
    return b.finish()


def build_l1ffn():
    b = Bld()
    I1 = 2816
    x = b.inp("x_in", (TPC, D0))
    wg = b.inp("wgT", (D0, I1))
    wu = b.inp("wuT", (D0, I1))
    wd = b.inp("wdT", (I1, D0))
    out = b.outp("o_out", (TPC, D0))
    b.mk_common(D0)
    sq = b.sq
    NB = I1 // 128
    wgb = b.sbuf("wgb", [128, 8, I1], BF16)
    wub = b.sbuf("wub", [128, 8, I1], BF16)
    wdblk = b.sbuf("wdblk", [128, D0], BF16)
    wstg = b.sbuf("wstg", [128, I1], F32)
    for kt in range(8):
        sq.dma("sync", wstg[:, 0:I1], wg[kt * 128:(kt + 1) * 128, :])
        sq.step("vector", lambda e, kt=kt: e.tensor_copy(out=wgb[:, kt, :], in_=wstg[:, 0:I1]))
        sq.dma("sync", wstg[:, 0:I1], wu[kt * 128:(kt + 1) * 128, :])
        sq.step("vector", lambda e, kt=kt: e.tensor_copy(out=wub[:, kt, :], in_=wstg[:, 0:I1]))

    bufs = mk_fp4_bufs(b, I1)
    xs = b.sbuf("xs", [128, D0], F32)
    lnf = b.sbuf("lnf", [128, D0], F32)
    xqb = b.sbuf("xqb", [128, D0], BF16)
    xT = b.sbuf("xT", [128, 8, 128], BF16)
    pg = b.psum("pg", [128, 128], F32)
    pu = b.psum("pu", [128, 128], F32)
    tg = b.sbuf("tg", [128, 128], F32)
    hrow = b.sbuf("hrow", [128, I1], F32)
    hqb = b.sbuf("hqb", [128, I1], BF16)
    hT = b.sbuf("hT", [128, NB, 128], BF16)
    pdn = [b.psum(f"pdn{i}", [128, 512], F32) for i in range(2)]
    osb = b.sbuf("osb", [128, D0], F32)
    gb = b.sbuf("gb", [64, 128], BF16)

    for t in range(TPC // 128):
        r0 = t * 128
        sq.dma("sync", xs[:, :], x[r0:r0 + 128, :])
        b.ln(xs[:, :], D0, out_f32=lnf[:, :])
        emit_fp4(b, lnf[:, :], xqb[:, :], D0, bufs)
        for kt in range(8):
            b.tp(xqb[:, kt * 128:(kt + 1) * 128], xT[:, kt, :])
        for nb in range(NB):
            for kt in range(8):
                sq.step("tensor", lambda e, nb=nb, kt=kt: e.matmul(
                    pg[:, :], lhsT=wgb[:, kt, nb * 128:(nb + 1) * 128], rhs=xT[:, kt, :], start=(kt == 0), stop=(kt == 7)))
                sq.step("tensor", lambda e, nb=nb, kt=kt: e.matmul(
                    pu[:, :], lhsT=wub[:, kt, nb * 128:(nb + 1) * 128], rhs=xT[:, kt, :], start=(kt == 0), stop=(kt == 7)))
            sq.step("scalar", lambda e: e.activation(out=tg[:, :], in_=pg[:, :], func=AF.Silu))
            sq.step("vector", lambda e: e.tensor_tensor(out=tg[:, :], in0=tg[:, :], in1=pu[:, :], op=ALU.mult))
            sq.step("vector", lambda e: e.tensor_copy(out=hqb[:, 0:128], in_=tg[:, :]))
            sq.step("tensor", lambda e: e.transpose(b.ptp[:, :], in_=hqb[:, 0:128], identity=b.ident[:, :]))
            sq.step("vector", lambda e, nb=nb: e.tensor_copy(out=hrow[:, nb * 128:(nb + 1) * 128], in_=b.ptp[:, :]))
        emit_fp4(b, hrow[:, 0:I1], hqb[:, 0:I1], I1, bufs)
        for nb in range(NB):
            b.tp(hqb[:, nb * 128:(nb + 1) * 128], hT[:, nb, :])
        for kt in range(NB):
            sq.dma("sync", wstg[:, 0:D0], wd[kt * 128:(kt + 1) * 128, :])
            sq.step("vector", lambda e: e.tensor_copy(out=wdblk[:, :], in_=wstg[:, 0:D0]))
            for half in range(2):
                sq.step("tensor", lambda e, half=half, kt=kt: e.matmul(
                    pdn[half][:, :], lhsT=hT[:, kt, :], rhs=wdblk[:, half * 512:(half + 1) * 512],
                    start=(kt == 0), stop=(kt == NB - 1)))
        for half in range(2):
            sq.step("vector", lambda e, half=half: e.tensor_tensor(
                out=osb[:, half * 512:(half + 1) * 512], in0=pdn[half][:, :], in1=xs[:, half * 512:(half + 1) * 512],
                op=ALU.add))
        sq.dma("gpsimd", out[r0:r0 + 128, :], osb[:, :])
    return b.finish()



ED = 2048


def build_m1():
    b = Bld()
    h = b.inp("h_in", (TPC, D0))
    g = b.inp("ln_g", (D0,))
    bt = b.inp("ln_b", (D0,))
    wT = b.inp("inwT", (D0, 2 * ED))
    uo = b.outp("uT_raw", (ED, TPC))
    zo = b.outp("zT", (ED, TPC))
    b.mk_common(D0)
    sq = b.sq
    grow = b.sbuf("grow", [1, D0], F32)
    brow = b.sbuf("brow", [1, D0], F32)
    sq.dma("sync", grow[:, :], g.unsqueeze(0))
    sq.dma("sync", brow[:, :], bt.unsqueeze(0))
    grep = b.sbuf("grep", [128, D0], F32)
    brep = b.sbuf("brep", [128, D0], F32)
    b.rep_row(grow, grep, D0)
    b.rep_row(brow, brep, D0)
    wf = b.sbuf("wf", [128, 4096], F32)
    wb = b.sbuf("wb", [128, 8, 4096], BF16)
    for kt in range(8):
        sq.dma("sync", wf[:, :], wT[kt * 128:(kt + 1) * 128, :])
        sq.step("vector", lambda e, kt=kt: e.tensor_copy(out=wb[:, kt, :], in_=wf[:, :]))
    ST = 512
    xs = b.sbuf("xs", [128, D0], F32)
    xnf = b.sbuf("xnf", [128, D0], F32)
    xnb = b.sbuf("xnb", [128, D0], BF16)
    xT = b.sbuf("xT", [128, 8, ST], BF16)
    pm = b.psum("pm", [128, ST], F32)
    ob = b.sbuf("ob", [128, ST], F32)
    for st in range(TPC // ST):
        base = st * ST
        for r in range(ST // 128):
            sq.dma("sync", xs[:, :], h[base + r * 128:base + (r + 1) * 128, :])
            b.ln(xs[:, :], D0, out_f32=xnf[:, :], out_bf=xnb[:, :], g_rep=grep[:, :], b_rep=brep[:, :])
            for kt in range(8):
                b.tp(xnb[:, kt * 128:(kt + 1) * 128], xT[:, kt, r * 128:(r + 1) * 128])
        for nb in range(32):
            for kt in range(8):
                sq.step("tensor", lambda e, nb=nb, kt=kt: e.matmul(
                    pm[:, :], lhsT=wb[:, kt, nb * 128:(nb + 1) * 128], rhs=xT[:, kt, :], start=(kt == 0), stop=(kt == 7)))
            sq.step("vector", lambda e: e.tensor_copy(out=ob[:, :], in_=pm[:, :]))
            if nb < 16:
                sq.dma("gpsimd", uo[nb * 128:(nb + 1) * 128, base:base + ST], ob[:, :])
            else:
                sq.dma("gpsimd", zo[(nb - 16) * 128:(nb - 15) * 128, base:base + ST], ob[:, :])
    return b.finish()


def build_m2a():
    b = Bld()
    u = b.inp("uT_in", (512, T0))
    cw = b.inp("convw", (4, 512))
    cb = b.inp("convb", (512,))
    uo = b.outp("uT_out", (512, T0))
    b.mk_common(128)
    sq = b.sq
    us = b.sbuf("us", [128, T0], F32)
    acc = b.sbuf("acc", [128, T0], F32)
    tmp = b.sbuf("tmp", [128, T0], F32)
    wc = b.sbuf("wc", [128, 4], F32)
    bc = b.sbuf("bc", [128, 1], F32)
    for pg in range(4):
        sq.dma("sync", us[:, :], u[pg * 128:(pg + 1) * 128, :])
        for j in range(4):
            sq.dma("sync", wc[:, j:j + 1], cw[j, pg * 128:(pg + 1) * 128].unsqueeze(1))
        sq.dma("sync", bc[:, :], cb[pg * 128:(pg + 1) * 128].unsqueeze(1))
        sq.step("vector", lambda e: e.tensor_scalar_mul(out=acc[:, :], in0=us[:, :], scalar1=wc[:, 3:4]))
        for j, sh in ((2, 1), (1, 2), (0, 3)):
            sq.step("vector", lambda e, j=j, sh=sh: e.tensor_scalar_mul(out=tmp[:, 0:T0 - sh], in0=us[:, 0:T0 - sh], scalar1=wc[:, j:j + 1]))
            sq.step("vector", lambda e, sh=sh: e.tensor_add(out=acc[:, sh:T0], in0=acc[:, sh:T0], in1=tmp[:, 0:T0 - sh]))
        sq.step("scalar", lambda e: e.activation(out=acc[:, :], in_=acc[:, :], func=AF.Silu, bias=bc[:, 0:1]))
        sq.dma("gpsimd", uo[pg * 128:(pg + 1) * 128, :], acc[:, :])
    return b.finish()


def build_m2b():
    b = Bld()
    u = b.inp("uT_cols", (ED, TPC))
    xp = b.inp("xprojT", (ED, 128))
    dp = b.inp("dtprojT", (64, ED))
    db = b.inp("dtb", (ED,))
    do = b.outp("deltaT", (ED, TPC))
    Bo = b.outp("BT", (32, TPC))
    Co = b.outp("CT", (32, TPC))
    b.mk_common(128)
    sq = b.sq
    xpf = b.sbuf("xpf", [128, 128], F32)
    xpb = b.sbuf("xpb", [128, 16, 128], BF16)
    for kt in range(16):
        sq.dma("sync", xpf[:, :], xp[kt * 128:(kt + 1) * 128, :])
        sq.step("vector", lambda e, kt=kt: e.tensor_copy(out=xpb[:, kt, :], in_=xpf[:, :]))
    dpf = b.sbuf("dpf", [64, ED], F32)
    dpb = b.sbuf("dpb", [64, ED], BF16)
    sq.dma("sync", dpf[:, :], dp)
    sq.step("vector", lambda e: e.tensor_copy(out=dpb[:, :], in_=dpf[:, :]))
    dbs = b.sbuf("dbs", [128, 16], F32)
    for eb in range(16):
        sq.dma("sync", dbs[:, eb:eb + 1], db[eb * 128:(eb + 1) * 128].unsqueeze(1))

    ST = 512
    uf = b.sbuf("uf", [128, ST], F32)
    ub = b.sbuf("ub", [128, 16, ST], BF16)
    px = b.psum("px", [128, ST], F32)
    dtb_ = b.sbuf("dtb_", [64, ST], BF16)
    bcs = b.sbuf("bcs", [64, ST], F32)
    pd = b.psum("pd", [128, ST], F32)
    sp_x = b.sbuf("sp_x", [128, ST], F32)
    sp_a = b.sbuf("sp_a", [128, ST], F32)
    sp_r = b.sbuf("sp_r", [128, ST], F32)
    for st in range(TPC // ST):
        base = st * ST
        for kt in range(16):
            sq.dma("sync", uf[:, :], u[kt * 128:(kt + 1) * 128, base:base + ST])
            sq.step("vector", lambda e, kt=kt: e.tensor_copy(out=ub[:, kt, :], in_=uf[:, :]))
        for kt in range(16):
            sq.step("tensor", lambda e, kt=kt: e.matmul(
                px[:, :], lhsT=xpb[:, kt, :], rhs=ub[:, kt, :], start=(kt == 0), stop=(kt == 15)))
        sq.step("vector", lambda e: e.tensor_copy(out=dtb_[:, :], in_=px[0:64, :]))
        sq.step("vector", lambda e: e.tensor_copy(out=bcs[:, :], in_=px[64:128, :]))
        sq.dma("gpsimd", Bo[:, base:base + ST], bcs[0:32, :])
        sq.dma("gpsimd", Co[:, base:base + ST], bcs[32:64, :])
        for eb in range(16):
            sq.step("tensor", lambda e, eb=eb: e.matmul(
                pd[:, :], lhsT=dpb[:, eb * 128:(eb + 1) * 128], rhs=dtb_[:, :], start=True, stop=True))
            sq.step("vector", lambda e, eb=eb: e.tensor_scalar_add(out=sp_x[:, :], in0=pd[:, :], scalar1=dbs[:, eb:eb + 1]))
            sq.step("scalar", lambda e: e.activation(out=sp_a[:, :], in_=sp_x[:, :], func=AF.Abs))
            sq.step("scalar", lambda e: e.activation(out=sp_a[:, :], in_=sp_a[:, :], func=AF.Exp, scale=-1.0))
            sq.step("scalar", lambda e: e.activation(out=sp_a[:, :], in_=sp_a[:, :], func=AF.Ln, bias=1.0))
            sq.step("vector", lambda e: e.tensor_scalar_max(out=sp_r[:, :], in0=sp_x[:, :], scalar1=0.0))
            sq.step("vector", lambda e: e.tensor_add(out=sp_r[:, :], in0=sp_r[:, :], in1=sp_a[:, :]))
            sq.dma("gpsimd", do[eb * 128:(eb + 1) * 128, base:base + ST], sp_r[:, :])
    return b.finish()


def build_m3():
    b = Bld()
    dl = b.inp("deltaT_sl", (512, T0))
    u = b.inp("uT_sl", (512, T0))
    Bt = b.inp("BT_b", (32, T0))
    Ct = b.inp("CT_b", (32, T0))
    Al = b.inp("A_log_sl", (512, 32))
    Dp = b.inp("Dp_sl", (512,))
    yo = b.outp("yT_sl", (512, T0))
    b.mk_common(128)
    sq = b.sq
    CH = 2048
    NCH = T0 // CH
    Asb = b.sbuf("Asb", [128, 4, 32], F32)
    for pg in range(4):
        sq.dma("sync", Asb[:, pg, :], Al[pg * 128:(pg + 1) * 128, :])
    sq.step("scalar", lambda e: e.activation(out=Asb[:, :, :], in_=Asb[:, :, :], func=AF.Exp))
    sq.step("vector", lambda e: e.tensor_scalar_mul(out=Asb[:, :, :], in0=Asb[:, :, :], scalar1=-1.0))
    Dcol = b.sbuf("Dcol", [128, 4], F32)
    for pg in range(4):
        sq.dma("sync", Dcol[:, pg:pg + 1], Dp[pg * 128:(pg + 1) * 128].unsqueeze(1))
    dls = b.sbuf("dls", [128, 4, CH], F32)
    us = b.sbuf("us", [128, 4, CH], F32)
    du = b.sbuf("du", [128, 4, CH], F32)
    ya = b.sbuf("ya", [128, 4, CH], F32)
    dA = b.sbuf("dA", [128, CH], F32)
    hs = b.sbuf("hs", [128, CH], F32)
    tmp = b.sbuf("tmp", [128, CH], F32)
    state = b.sbuf("state", [128, 4, 32], F32)
    rstg = b.sbuf("rstg", [1, CH], F32)
    pB = b.psum("pB", [128, 1024], F32)
    pC = b.psum("pC", [128, 1024], F32)
    Brep = b.sbuf("Brep", [128, CH], F32)
    Crep = b.sbuf("Crep", [128, CH], F32)

    for ch in range(NCH):
        c0 = ch * CH
        for pg in range(4):
            sq.dma("sync", dls[:, pg, :], dl[pg * 128:(pg + 1) * 128, c0:c0 + CH])
            sq.dma("sync", us[:, pg, :], u[pg * 128:(pg + 1) * 128, c0:c0 + CH])
        sq.step("vector", lambda e: e.tensor_mul(out=du[:, :, :], in0=dls[:, :, :], in1=us[:, :, :]))
        sq.step("vector", lambda e: e.memset(ya[:, :, :], 0.0))
        for n in range(32):
            sq.dma("sync", rstg[:, :], Bt[n, c0:c0 + CH].unsqueeze(0))
            for hf in range(2):
                for w0 in range(0, 1024, 512):
                    sq.step("tensor", lambda e, hf=hf, w0=w0: e.matmul(
                        pB[:, w0:w0 + 512], lhsT=b.ones_f[0:1, :], rhs=rstg[0:1, hf * 1024 + w0:hf * 1024 + w0 + 512],
                        start=True, stop=True))
                sq.step("vector", lambda e, hf=hf: e.tensor_copy(out=Brep[:, hf * 1024:(hf + 1) * 1024], in_=pB[:, :]))
            sq.dma("sync", rstg[:, :], Ct[n, c0:c0 + CH].unsqueeze(0))
            for hf in range(2):
                for w0 in range(0, 1024, 512):
                    sq.step("tensor", lambda e, hf=hf, w0=w0: e.matmul(
                        pC[:, w0:w0 + 512], lhsT=b.ones_f[0:1, :], rhs=rstg[0:1, hf * 1024 + w0:hf * 1024 + w0 + 512],
                        start=True, stop=True))
                sq.step("vector", lambda e, hf=hf: e.tensor_copy(out=Crep[:, hf * 1024:(hf + 1) * 1024], in_=pC[:, :]))
            for pg in range(4):
                sq.step("scalar", lambda e, pg=pg, n=n: e.activation(
                    out=dA[:, :], in_=dls[:, pg, :], func=AF.Exp, scale=Asb[:, pg, n:n + 1]))
                sq.step("vector", lambda e, pg=pg: e.tensor_mul(out=tmp[:, :], in0=du[:, pg, :], in1=Brep[:, :]))
                if ch == 0:
                    sq.step("vector", lambda e: e.tensor_tensor_scan(
                        out=hs[:, :], data0=dA[:, :], data1=tmp[:, :], initial=0.0, op0=ALU.mult, op1=ALU.add))
                else:
                    sq.step("vector", lambda e, pg=pg, n=n: e.tensor_tensor_scan(
                        out=hs[:, :], data0=dA[:, :], data1=tmp[:, :], initial=state[:, pg, n:n + 1],
                        op0=ALU.mult, op1=ALU.add))
                sq.step("vector", lambda e, pg=pg, n=n: e.tensor_copy(out=state[:, pg, n:n + 1], in_=hs[:, CH - 1:CH]))
                sq.step("vector", lambda e: e.tensor_mul(out=tmp[:, :], in0=hs[:, :], in1=Crep[:, :]))
                sq.step("vector", lambda e, pg=pg: e.tensor_add(out=ya[:, pg, :], in0=ya[:, pg, :], in1=tmp[:, :]))
        for pg in range(4):
            sq.step("vector", lambda e, pg=pg: e.scalar_tensor_tensor(
                out=ya[:, pg, :], in0=us[:, pg, :], scalar=Dcol[:, pg:pg + 1], in1=ya[:, pg, :],
                op0=ALU.mult, op1=ALU.add))
            sq.dma("gpsimd", yo[pg * 128:(pg + 1) * 128, c0:c0 + CH], ya[:, pg, :])
    return b.finish()


def build_m4():
    b = Bld()
    y = b.inp("yT_cols", (ED, TPC))
    z = b.inp("zT_cols", (ED, TPC))
    h = b.inp("h_in", (TPC, D0))
    wT = b.inp("outwT", (ED, D0))
    out = b.outp("h_out", (TPC, D0))
    b.mk_common(D0)
    sq = b.sq
    wf = b.sbuf("wf", [128, D0], F32)
    wb = b.sbuf("wb", [128, 16, D0], BF16)
    for kt in range(16):
        sq.dma("sync", wf[:, :], wT[kt * 128:(kt + 1) * 128, :])
        sq.step("vector", lambda e, kt=kt: e.tensor_copy(out=wb[:, kt, :], in_=wf[:, :]))
    ST = 512
    ys = b.sbuf("ys", [128, 16, ST], F32)
    zs = b.sbuf("zs", [128, 16, ST], F32)
    gb = b.sbuf("gb", [128, 16, ST], BF16)
    pm = [b.psum(f"pm{i}", [128, 512], F32) for i in range(2)]
    hsb = b.sbuf("hsb", [128, D0], F32)
    osb = b.sbuf("osb", [128, D0], F32)
    for st in range(TPC // ST):
        base = st * ST
        for kt in range(16):
            sq.dma("sync", ys[:, kt, :], y[kt * 128:(kt + 1) * 128, base:base + ST])
            sq.dma("sync", zs[:, kt, :], z[kt * 128:(kt + 1) * 128, base:base + ST])
        sq.step("scalar", lambda e: e.activation(out=zs[:, :, :], in_=zs[:, :, :], func=AF.Silu))
        sq.step("vector", lambda e: e.tensor_mul(out=gb[:, :, :], in0=ys[:, :, :], in1=zs[:, :, :]))
        for r in range(ST // 128):
            for half in range(2):
                for kt in range(16):
                    sq.step("tensor", lambda e, half=half, kt=kt, r=r: e.matmul(
                        pm[half][:, :], lhsT=gb[:, kt, r * 128:(r + 1) * 128],
                        rhs=wb[:, kt, half * 512:(half + 1) * 512], start=(kt == 0), stop=(kt == 15)))
            sq.dma("sync", hsb[:, :], h[base + r * 128:base + (r + 1) * 128, :])
            for half in range(2):
                sq.step("vector", lambda e, half=half: e.tensor_tensor(
                    out=osb[:, half * 512:(half + 1) * 512], in0=pm[half][:, :],
                    in1=hsb[:, half * 512:(half + 1) * 512], op=ALU.add))
            sq.dma("gpsimd", out[base + r * 128:base + (r + 1) * 128, :], osb[:, :])
    return b.finish()


def build_l2fin():
    b = Bld()
    x = b.inp("x_in", (TPC, D0))
    h = b.inp("h_in", (TPC, D0))
    g = b.inp("fn_g", (D0,))
    bt = b.inp("fn_b", (D0,))
    out = b.outp("o_out", (TPC, D0))
    b.mk_common(D0)
    sq = b.sq
    grow = b.sbuf("grow", [1, D0], F32)
    brow = b.sbuf("brow", [1, D0], F32)
    sq.dma("sync", grow[:, :], g.unsqueeze(0))
    sq.dma("sync", brow[:, :], bt.unsqueeze(0))
    grep = b.sbuf("grep", [128, D0], F32)
    brep = b.sbuf("brep", [128, D0], F32)
    b.rep_row(grow, grep, D0)
    b.rep_row(brow, brep, D0)
    hs = b.sbuf("hs", [128, D0], F32)
    xs = b.sbuf("xs", [128, D0], F32)
    lnf = b.sbuf("lnf", [128, D0], F32)
    for t in range(TPC // 128):
        r0 = t * 128
        sq.dma("sync", hs[:, :], h[r0:r0 + 128, :])
        sq.dma("sync", xs[:, :], x[r0:r0 + 128, :])
        b.ln(hs[:, :], D0, out_f32=lnf[:, :], g_rep=grep[:, :], b_rep=brep[:, :])
        sq.step("vector", lambda e: e.tensor_add(out=lnf[:, :], in0=lnf[:, :], in1=xs[:, :]))
        sq.dma("gpsimd", out[r0:r0 + 128, :], lnf[:, :])
    return b.finish()




def build_l3attn():
    b = Bld()
    x = b.inp("x_sl", (T0, D0))
    wq = b.inp("wqh", (D0, 256))
    wk = b.inp("wkh", (D0, 256))
    wv = b.inp("wvh", (D0, 256))
    bq = b.inp("bqh", (256,))
    bk = b.inp("bkh", (256,))
    bv = b.inp("bvh", (256,))
    nw = b.inp("nw4", (256,))
    cost = b.inp("cost", (T0, 256))
    sint = b.inp("sint", (T0, 256))
    zo = b.outp("z_sh", (T0, 256))

    b.mk_common(D0)
    sq = b.sq
    tri = b.sbuf("tri", [128, 128], F32)
    sq.dma("sync", tri[:, :], b.const("tric", _TRI))
    wsb = {}
    for nm, w in (("q", wq), ("k", wk), ("v", wv)):
        f = b.sbuf(f"w{nm}f", [128, 8, 256], F32)
        bb = b.sbuf(f"w{nm}b", [128, 8, 256], BF16)
        for kt in range(8):
            sq.dma("sync", f[:, kt, :], w[kt * 128:(kt + 1) * 128, :])
        sq.step("vector", lambda e, f=f, bb=bb: e.tensor_copy(out=bb[:, :, :], in_=f[:, :, :]))
        wsb[nm] = bb
    brep = {}
    for nm, bias in (("q", bq), ("k", bk), ("v", bv)):
        row = b.sbuf(f"br{nm}", [1, 256], F32)
        sq.dma("sync", row[:, :], bias.unsqueeze(0))
        rep = b.sbuf(f"brep{nm}", [128, 256], F32)
        b.rep_row(row, rep, 256)
        brep[nm] = rep
    nwrow = b.sbuf("nwrow", [1, 256], F32)
    sq.dma("sync", nwrow[:, :], nw.unsqueeze(0))
    nwrep = b.sbuf("nwrep", [128, 256], F32)
    b.rep_row(nwrow, nwrep, 256)

    qT = b.sbuf("qT", [128, 2, 64, 128], BF16)
    kTs = b.sbuf("kTs", [128, 2, 64, 128], BF16)
    vs = b.sbuf("vs", [128, 4, 64, 64], BF16)
    xs = b.sbuf("xs", [128, D0], F32)
    xnb = b.sbuf("xnb", [128, D0], BF16)
    xT = b.sbuf("xT", [128, 8, 128], BF16)
    pq = b.psum("pq", [128, 256], F32)
    qf = b.sbuf("qf", [128, 256], F32)
    sqv = b.sbuf("sqv", [128, 256], F32)
    v4 = b.sbuf("v4", [128, 4], F32)
    sd4 = b.sbuf("sd4", [128, 4], F32)
    r4 = b.sbuf("r4", [128, 4], F32)
    ct = b.sbuf("ct", [128, 256], F32)
    st_ = b.sbuf("st_", [128, 256], F32)
    rt = b.sbuf("rt", [128, 256], F32)
    tmp32 = b.sbuf("tmp32", [128, 4, 32], F32)
    qb2 = b.sbuf("qb2", [128, 256], BF16)
    EPSR = 1.1920929e-07

    def rms_rope(dst_bf):
        sq.step("scalar", lambda e: e.activation(out=sqv[:, :], in_=qf[:, :], func=AF.Square))
        sq.step("vector", lambda e: e.tensor_reduce(out=v4[:, :], in_=sqv[:, :].rearrange("p (h d) -> p h d", d=64),
                                                    axis=AX.X, op=ALU.add))
        sq.step("vector", lambda e: e.tensor_scalar(out=v4[:, :], in0=v4[:, :], scalar1=1.0 / 64, scalar2=EPSR,
                                                    op0=ALU.mult, op1=ALU.add))
        sq.step("scalar", lambda e: e.activation(out=sd4[:, :], in_=v4[:, :], func=AF.Sqrt))
        sq.step("vector", lambda e: e.reciprocal(out=r4[:, :], in_=sd4[:, :]))
        for h in range(4):
            sq.step("vector", lambda e, h=h: e.tensor_scalar_mul(out=qf[:, h * 64:(h + 1) * 64], in0=qf[:, h * 64:(h + 1) * 64],
                                                                 scalar1=r4[:, h:h + 1]))
        sq.step("vector", lambda e: e.tensor_mul(out=qf[:, :], in0=qf[:, :], in1=nwrep[:, :]))
        sq.step("vector", lambda e: e.tensor_mul(out=rt[:, :], in0=qf[:, :], in1=ct[:, :]))
        qv = qf[:, :].rearrange("p (a two) -> p a two", two=2)
        rv = rt[:, :].rearrange("p (a two) -> p a two", two=2)
        sv = st_[:, :].rearrange("p (a two) -> p a two", two=2)
        sq.step("vector", lambda e: e.tensor_mul(out=tmp32[:, :, :].rearrange("p a b -> p (a b)")[:, 0:128], in0=qv[:, :, 1], in1=sv[:, :, 0]))
        sq.step("vector", lambda e: e.tensor_sub(out=rv[:, :, 0], in0=rv[:, :, 0], in1=tmp32[:, :, :].rearrange("p a b -> p (a b)")[:, 0:128]))
        sq.step("vector", lambda e: e.tensor_mul(out=tmp32[:, :, :].rearrange("p a b -> p (a b)")[:, 0:128], in0=qv[:, :, 0], in1=sv[:, :, 1]))
        sq.step("vector", lambda e: e.tensor_add(out=rv[:, :, 1], in0=rv[:, :, 1], in1=tmp32[:, :, :].rearrange("p a b -> p (a b)")[:, 0:128]))
        sq.step("vector", lambda e: e.tensor_copy(out=dst_bf, in_=rt[:, :]))

    for j in range(64):
        r0 = j * 128
        sq.dma("sync", xs[:, :], x[r0:r0 + 128, :])
        sq.dma("sync", ct[:, :], cost[r0:r0 + 128, :])
        sq.dma("sync", st_[:, :], sint[r0:r0 + 128, :])
        b.ln(xs[:, :], D0, out_bf=xnb[:, :])
        for kt in range(8):
            b.tp(xnb[:, kt * 128:(kt + 1) * 128], xT[:, kt, :])
        for nm, dstT in (("q", qT), ("k", kTs)):
            for kt in range(8):
                sq.step("tensor", lambda e, nm=nm, kt=kt: e.matmul(
                    pq[:, :], lhsT=xT[:, kt, :], rhs=wsb[nm][:, kt, :], start=(kt == 0), stop=(kt == 7)))
            sq.step("vector", lambda e, nm=nm: e.tensor_tensor(out=qf[:, :], in0=pq[:, :], in1=brep[nm][:, :], op=ALU.add))
            rms_rope(qb2[:, :])
            for tb in range(2):
                sq.step("tensor", lambda e, tb=tb: e.transpose(b.ptp[:, :], in_=qb2[:, tb * 128:(tb + 1) * 128],
                                                               identity=b.ident[:, :]))
                sq.step("vector", lambda e, tb=tb, j=j, dstT=dstT: e.tensor_copy(out=dstT[:, tb, j, :], in_=b.ptp[:, :]))
        for kt in range(8):
            sq.step("tensor", lambda e, kt=kt: e.matmul(
                pq[:, :], lhsT=xT[:, kt, :], rhs=wsb["v"][:, kt, :], start=(kt == 0), stop=(kt == 7)))
        sq.step("vector", lambda e: e.tensor_tensor(out=qf[:, :], in0=pq[:, :], in1=brep["v"][:, :], op=ALU.add))
        sq.step("vector", lambda e, j=j: e.tensor_copy(out=vs[:, :, j, :], in_=qf[:, :].rearrange("p (a b) -> p a b", b=64)))

    pst = b.psum("pst", [128, 512], F32)
    pz = b.psum("pz", [64, 512], F32)
    Pb = b.sbuf("Pb", [128, 512], BF16)
    zbf = b.sbuf("zbf", [64, 512], BF16)
    ptp64 = b.psum("ptp64", [128, 64], BF16)
    zf = b.sbuf("zf", [128, 64], F32)
    for h in range(4):
        pb = (h % 2) * 64
        tb = h // 2
        for i0 in range(0, 64, 4):
            ni = min(4, 64 - i0)
            nfirst = True
            for jj in range(0, i0 + ni):
                if jj < i0:
                    w = ni * 128
                    off = 0
                else:
                    w = (i0 + ni - jj) * 128
                    off = (jj - i0) * 128
                sq.step("tensor", lambda e, jj=jj, i0=i0, off=off, w=w, pb=pb, tb=tb: e.matmul(
                    pst[:, off:off + w], lhsT=kTs[pb:pb + 64, tb, jj, :],
                    rhs=qT[pb:pb + 64, tb, i0:i0 + ni, :].rearrange("p a b -> p (a b)")[:, off:off + w],
                    start=True, stop=True))
                sq.step("scalar", lambda e, off=off, w=w: e.activation(
                    out=Pb[:, off:off + w], in_=pst[:, off:off + w], func=AF.Square, scale=1.0 / 64))
                if jj >= i0:
                    sq.step("vector", lambda e, off=off: e.tensor_tensor(
                        out=Pb[:, off:off + 128], in0=Pb[:, off:off + 128], in1=tri[:, :], op=ALU.mult))
                sq.step("tensor", lambda e, jj=jj, off=off, w=w, h=h, nfirst=nfirst, i0=i0, ni=ni: e.matmul(
                    pz[:, off:off + w], lhsT=vs[:, h, jj, :], rhs=Pb[:, off:off + w],
                    start=nfirst, stop=(jj == i0 + ni - 1), skip_group_check=True))
                nfirst = False
            sq.step("vector", lambda e, ni=ni: e.tensor_copy(out=zbf[:, 0:ni * 128], in_=pz[:, 0:ni * 128]))
            for ii in range(ni):
                sq.step("tensor", lambda e, ii=ii: e.transpose(ptp64[:, :], in_=zbf[:, ii * 128:(ii + 1) * 128],
                                                               identity=b.ident[0:64, 0:64]))
                sq.step("vector", lambda e: e.tensor_copy(out=zf[:, :], in_=ptp64[:, :]))
                sq.dma("gpsimd", zo[(i0 + ii) * 128:(i0 + ii + 1) * 128, h * 64:(h + 1) * 64], zf[:, :])
    return b.finish()



def _get(name, fn):
    if name not in _prog_cache:
        _prog_cache[name] = fn()
    return _prog_cache[name]


def _run(name, fn, in_maps):
    nc = _get(name, fn)
    res = run_bass_kernel_spmd(nc, in_maps, core_ids=list(range(NC)))
    return res.results


def _tok_maps(key_arrs):
    maps = []
    for c in range(NC):
        m = {}
        for k, (arr, shard) in key_arrs.items():
            if shard == 0:
                m[k] = np.ascontiguousarray(arr[c * TPC:(c + 1) * TPC])
            else:
                m[k] = arr
        maps.append(m)
    return maps


def _scl(s):
    return np.ascontiguousarray(s.reshape(-1, 128).T)


def kernel(_collect=None, **inp):
    inp = {k: np.asarray(v, dtype=np.float32) for k, v in inp.items()}
    x2d = np.ascontiguousarray(inp["L0_attn_x"].reshape(M0, D0))

    wqT = np.ascontiguousarray(inp["L0_attn_Wq"].T)
    wkT = np.ascontiguousarray(inp["L0_attn_Wk"].T)
    wvT = np.ascontiguousarray(inp["L0_attn_Wv"].T)
    maps = []
    for c in range(NC):
        bi = c // 4
        h0 = (c % 4) * 4
        sl = slice(h0 * 64, (h0 + 4) * 64)
        maps.append({"x_sl": np.ascontiguousarray(x2d[bi * T0:(bi + 1) * T0]),
                     "wqh": np.ascontiguousarray(wqT[:, sl]), "wkh": np.ascontiguousarray(wkT[:, sl]),
                     "wvh": np.ascontiguousarray(wvT[:, sl]),
                     "bqh": np.ascontiguousarray(inp["L0_attn_bq"][sl]),
                     "bkh": np.ascontiguousarray(inp["L0_attn_bk"][sl]),
                     "bvh": np.ascontiguousarray(inp["L0_attn_bv"][sl])})
    res = _run("lin_attn", build_lin_attn, maps)
    attn0 = np.empty((B0, T0, 16, 64), np.float32)
    for c in range(NC):
        bi = c // 4
        h0 = (c % 4) * 4
        attn0[bi, :, h0:h0 + 4, :] = res[c]["o_sh"].reshape(T0, 4, 64)
    attn0 = np.ascontiguousarray(attn0.reshape(M0, D0))
    if _collect is not None:
        _collect['attn0'] = attn0

    woT = np.ascontiguousarray(inp["L0_attn_Wo"].T)
    res = _run("proj_res_w", lambda: build_proj_res(D0, D0, True, False, False, True),
               _tok_maps({"a_in": (attn0, 0), "x_in": (x2d, 0), "wT": (woT, None),
                          "bias": (inp["L0_attn_bo"], None)}))
    x1 = np.concatenate([r["o_out"] for r in res], 0)
    if _collect is not None:
        _collect['x1'] = x1
        if _collect.get('stop') == 'x1':
            return x1

    def ffn_tern(xin, pre):
        arrs = {"x_in": (xin, 0),
                "w1T": (np.ascontiguousarray(inp[pre + "w1_w"].T), None),
                "w2T": (np.ascontiguousarray(inp[pre + "w2_w"].T), None),
                "h1T": (np.ascontiguousarray(inp[pre + "h1_w"].T), None),
                "h2T": (np.ascontiguousarray(inp[pre + "h2_w"].T), None),
                "w3T": (np.ascontiguousarray(inp[pre + "w3_w"].T), None),
                "s1": (_scl(inp[pre + "w1_s"]), None), "s2": (_scl(inp[pre + "w2_s"]), None),
                "sh1": (_scl(inp[pre + "h1_s"]), None), "sh2": (_scl(inp[pre + "h2_s"]), None),
                "s3": (_scl(inp[pre + "w3_s"]), None)}
        res = _run("ffn_ternary", build_ffn_ternary, _tok_maps(arrs))
        return np.concatenate([r["o_out"] for r in res], 0)

    x2 = ffn_tern(x1, "L0_ffn_")
    if _collect is not None:
        _collect['x2'] = x2
        if _collect.get('stop') == 'x2':
            return x2

    res = _run("l1qkv", build_l1qkv,
               _tok_maps({"x_in": (x2, 0),
                          "wqT": (np.ascontiguousarray(inp["L1_attn_Wq"].T), None),
                          "wkT": (np.ascontiguousarray(inp["L1_attn_Wk"].T), None),
                          "wvT": (np.ascontiguousarray(inp["L1_attn_Wv"].T), None),
                          "bq": (inp["L1_attn_bq"], None), "bk": (inp["L1_attn_bk"], None),
                          "bv": (inp["L1_attn_bv"], None)}))
    kT_full = np.concatenate([r["kT_sh"] for r in res], 1)
    v_full = np.concatenate([r["v_sh"] for r in res], 0)
    qT_shs = [r["qT_sh"] for r in res]
    maps = []
    for c in range(NC):
        maps.append({"qT": np.ascontiguousarray(qT_shs[c]), "kT_full": kT_full, "v_full": v_full,
                     "x_in": np.ascontiguousarray(x2[c * TPC:(c + 1) * TPC])})
    res = _run("l1flash", build_l1flash, maps)
    x3 = np.concatenate([r["o_out"] for r in res], 0)
    if _collect is not None:
        _collect['x3'] = x3
        if _collect.get('stop') == 'x3':
            return x3

    def quantT(w, name):
        R, W = w.shape
        ntile_core = -(-R // (128 * NC))
        Rpad = ntile_core * 128 * NC
        wp = np.zeros((Rpad, W), np.float32)
        wp[:R] = w
        maps = [{"w_in": np.ascontiguousarray(wp[c * ntile_core * 128:(c + 1) * ntile_core * 128])}
                for c in range(NC)]
        res = _run(f"l1wq_{W}_{ntile_core}", lambda: build_l1wq(ntile_core, W), maps)
        full = np.concatenate([r["wqT_out"] for r in res], 1)
        return np.ascontiguousarray(full[:, :R])

    wgT = quantT(inp["L1_ffn_w_gate"], "wg")
    wuT = quantT(inp["L1_ffn_w_up"], "wu")
    wdT = quantT(inp["L1_ffn_w_down"], "wd")
    res = _run("l1ffn", build_l1ffn,
               _tok_maps({"x_in": (x3, 0), "wgT": (wgT, None), "wuT": (wuT, None), "wdT": (wdT, None)}))
    x4 = np.concatenate([r["o_out"] for r in res], 0)
    if _collect is not None:
        _collect['x4'] = x4
        if _collect.get('stop') == 'x4':
            return x4

    res = _run("proj_res_wi", lambda: build_proj_res(D0, D0, False, True, False, True),
               _tok_maps({"a_in": (x4, 0), "wT": (np.ascontiguousarray(inp["L2_attn_Wi"].T), None),
                          "bias": (inp["L2_attn_bi"], None)}))
    h = np.concatenate([r["o_out"] for r in res], 0)
    if _collect is not None:
        _collect['h0'] = h

    convw_t = np.ascontiguousarray(np.transpose(inp["L2_attn_conv_w"][:, :, 0, :], (0, 2, 1)))
    for i in range(4):
        res = _run("m1", build_m1,
                   _tok_maps({"h_in": (h, 0),
                              "ln_g": (inp["L2_attn_ln_g"][i], None), "ln_b": (inp["L2_attn_ln_b"][i], None),
                              "inwT": (np.ascontiguousarray(inp["L2_attn_in_w"][i].T), None)}))
        uT_raw = np.concatenate([r["uT_raw"] for r in res], 1)
        zT = np.concatenate([r["zT"] for r in res], 1)
        maps = []
        for c in range(NC):
            bi = c // 4
            es = (c % 4) * 512
            maps.append({"uT_in": np.ascontiguousarray(uT_raw[es:es + 512, bi * T0:(bi + 1) * T0]),
                         "convw": np.ascontiguousarray(convw_t[i][:, es:es + 512]),
                         "convb": np.ascontiguousarray(inp["L2_attn_conv_b"][i][es:es + 512])})
        res = _run("m2a", build_m2a, maps)
        uT_conv = np.empty((ED, M0), np.float32)
        for c in range(NC):
            bi = c // 4
            es = (c % 4) * 512
            uT_conv[es:es + 512, bi * T0:(bi + 1) * T0] = res[c]["uT_out"]
        maps = []
        xprojT = np.ascontiguousarray(inp["L2_attn_xproj_w"][i].T)
        dtprojT = np.ascontiguousarray(inp["L2_attn_dtproj_w"][i].T)
        for c in range(NC):
            maps.append({"uT_cols": np.ascontiguousarray(uT_conv[:, c * TPC:(c + 1) * TPC]),
                         "xprojT": xprojT, "dtprojT": dtprojT,
                         "dtb": np.ascontiguousarray(inp["L2_attn_dtproj_b"][i])})
        res = _run("m2b", build_m2b, maps)
        deltaT = np.concatenate([r["deltaT"] for r in res], 1)
        BT = np.concatenate([r["BT"] for r in res], 1)
        CT = np.concatenate([r["CT"] for r in res], 1)
        maps = []
        for c in range(NC):
            bi = c // 4
            es = (c % 4) * 512
            cols = slice(bi * T0, (bi + 1) * T0)
            maps.append({"deltaT_sl": np.ascontiguousarray(deltaT[es:es + 512, cols]),
                         "uT_sl": np.ascontiguousarray(uT_conv[es:es + 512, cols]),
                         "BT_b": np.ascontiguousarray(BT[:, cols]), "CT_b": np.ascontiguousarray(CT[:, cols]),
                         "A_log_sl": np.ascontiguousarray(inp["L2_attn_A_log"][i][es:es + 512]),
                         "Dp_sl": np.ascontiguousarray(inp["L2_attn_Dp"][i][es:es + 512])})
        res = _run("m3", build_m3, maps)
        yT = np.empty((ED, M0), np.float32)
        for c in range(NC):
            bi = c // 4
            es = (c % 4) * 512
            yT[es:es + 512, bi * T0:(bi + 1) * T0] = res[c]["yT_sl"]
        maps = []
        outwT = np.ascontiguousarray(inp["L2_attn_out_w"][i].T)
        for c in range(NC):
            cols = slice(c * TPC, (c + 1) * TPC)
            maps.append({"yT_cols": np.ascontiguousarray(yT[:, cols]),
                         "zT_cols": np.ascontiguousarray(zT[:, cols]),
                         "h_in": np.ascontiguousarray(h[cols]), "outwT": outwT})
        res = _run("m4", build_m4, maps)
        h = np.concatenate([r["h_out"] for r in res], 0)
        if _collect is not None:
            _collect[f'h{i+1}'] = h
            _collect[f'uT_conv{i}'] = uT_conv
            _collect[f'deltaT{i}'] = deltaT
            _collect[f'yT{i}'] = yT
            _collect[f'zT{i}'] = zT

    res = _run("l2fin", build_l2fin,
               _tok_maps({"x_in": (x4, 0), "h_in": (h, 0), "fn_g": (inp["L2_attn_fn_g"], None),
                          "fn_b": (inp["L2_attn_fn_b"], None)}))
    x5 = np.concatenate([r["o_out"] for r in res], 0)
    if _collect is not None:
        _collect['x5'] = x5
        if _collect.get('stop') == 'x5':
            return x5

    rwT = np.ascontiguousarray(inp["L2_ffn_router_w"].T)
    maps = []
    for c in range(NC):
        maps.append({"x_in": np.ascontiguousarray(x5[c * 256:(c + 1) * 256]), "rwT": rwT,
                     "rb": inp["L2_ffn_router_b"],
                     "experts_inter": inp["L2_ffn_experts_inter"], "experts_out": inp["L2_ffn_experts_out"]})
    res = _run("moe", build_moe3, maps)
    x6 = x5.copy()
    for c in range(NC):
        x6[c * 256:(c + 1) * 256] = res[c]["o_out"]
    if _collect is not None:
        _collect['x6'] = x6
        if _collect.get('stop') == 'x6':
            return x6

    inv_freq = 1.0 / (10000.0 ** (np.arange(0, 64, 2, dtype=np.float32) / 64))
    t_ = np.arange(T0, dtype=np.float32)
    freqs = np.outer(t_, inv_freq)
    emb = np.concatenate([freqs, freqs], -1)
    cos_t = np.ascontiguousarray(np.tile(np.cos(emb).astype(np.float32), (1, 4)))
    sin_t = np.ascontiguousarray(np.tile(np.sin(emb).astype(np.float32), (1, 4)))
    wqT3 = np.ascontiguousarray(inp["L3_attn_Wq"].T)
    wkT3 = np.ascontiguousarray(inp["L3_attn_Wk"].T)
    wvT3 = np.ascontiguousarray(inp["L3_attn_Wv"].T)
    nw4 = np.ascontiguousarray(np.tile(inp["L3_attn_norm_w"], 4))
    maps = []
    for c in range(NC):
        bi = c // 4
        h0 = (c % 4) * 4
        sl = slice(h0 * 64, (h0 + 4) * 64)
        maps.append({"x_sl": np.ascontiguousarray(x6[bi * T0:(bi + 1) * T0]),
                     "wqh": np.ascontiguousarray(wqT3[:, sl]), "wkh": np.ascontiguousarray(wkT3[:, sl]),
                     "wvh": np.ascontiguousarray(wvT3[:, sl]),
                     "bqh": np.ascontiguousarray(inp["L3_attn_bq"][sl]),
                     "bkh": np.ascontiguousarray(inp["L3_attn_bk"][sl]),
                     "bvh": np.ascontiguousarray(inp["L3_attn_bv"][sl]),
                     "nw4": nw4, "cost": cos_t, "sint": sin_t})
    res = _run("l3attn", build_l3attn, maps)
    z = np.empty((B0, T0, 16, 64), np.float32)
    for c in range(NC):
        bi = c // 4
        h0 = (c % 4) * 4
        z[bi, :, h0:h0 + 4, :] = res[c]["z_sh"].reshape(T0, 4, 64)
    z = np.ascontiguousarray(z.reshape(M0, D0))
    if _collect is not None:
        _collect['z'] = z

    res = _run("proj_res_l3", lambda: build_proj_res(D0, D0, True, False, True, False),
               _tok_maps({"a_in": (z, 0), "x_in": (x6, 0),
                          "wT": (np.ascontiguousarray(inp["L3_attn_Wo"].T), None)}))
    x7 = np.concatenate([r["o_out"] for r in res], 0)
    if _collect is not None:
        _collect['x7'] = x7
        if _collect.get('stop') == 'x7':
            return x7

    x8 = ffn_tern(x7, "L3_ffn_")
    return x8.reshape(B0, T0, D0).astype(np.float32)



def build_moe3():
    b = Bld()
    ROWS = 256
    x = b.inp("x_in", (ROWS, D0))
    rw = b.inp("rwT", (D0, 8))
    rb = b.inp("rb", (8,))
    wi = b.inp("experts_inter", (8, D0, 4096))
    wo = b.inp("experts_out", (8, 4096, D0))
    out = b.outp("o_out", (ROWS, D0))
    b.mk_common(D0)
    sq = b.sq
    rwf = b.sbuf("rwf", [128, 8, 8], F32)
    rwb = b.sbuf("rwb", [128, 8, 8], BF16)
    for kt in range(8):
        sq.dma("sync", rwf[:, kt, :], rw[kt * 128:(kt + 1) * 128, :])
    sq.step("vector", lambda e: e.tensor_copy(out=rwb[:, :, :], in_=rwf[:, :, :]))
    rbrow = b.sbuf("rbrow", [1, 8], F32)
    sq.dma("sync", rbrow[:, :], rb.unsqueeze(0))
    rbrep = b.sbuf("rbrep", [128, 8], F32)
    b.rep_row(rbrow, rbrep, 8)

    xs = [b.sbuf(f"xs{r}", [128, D0], F32) for r in range(2)]
    lnf = [b.sbuf(f"lnf{r}", [128, D0], F32) for r in range(2)]
    lnb = b.sbuf("lnb", [128, D0], BF16)
    xT = b.sbuf("xT", [128, 8, 256], BF16)
    plog = b.psum("plog", [128, 8], F32)
    lg = b.sbuf("lg", [128, 8], F32)
    mx = b.sbuf("mx", [128, 1], F32)
    ex = b.sbuf("ex", [128, 8], F32)
    sm = b.sbuf("sm", [128, 1], F32)
    rcs = b.sbuf("rcs", [128, 1], F32)
    wgt = [b.sbuf(f"wgt{r}", [128, 8], F32) for r in range(2)]
    xsc = b.sbuf("xsc", [128, D0], BF16)
    xscT = b.sbuf("xscT", [128, 8, 256], BF16)
    wstg = b.sbuf("wstg", [128, 4096], F32)
    wstgb = b.sbuf("wstgb", [128, 4096], BF16)
    pi = b.psum("pi", [128, 256], F32)
    iT = b.sbuf("iT", [128, 32, 256], BF16)
    po = b.psum("po", [128, 8, 256], F32)
    obf = b.sbuf("obf", [128, 256], BF16)
    osb = [b.sbuf(f"osb{r}", [128, D0], F32) for r in range(2)]

    for r in range(2):
        sq.dma("sync", xs[r][:, :], x[r * 128:(r + 1) * 128, :])
        b.ln(xs[r][:, :], D0, out_f32=lnf[r][:, :], out_bf=lnb[:, :])
        for kt in range(8):
            b.tp(lnb[:, kt * 128:(kt + 1) * 128], xT[:, kt, r * 128:(r + 1) * 128])
        for kt in range(8):
            sq.step("tensor", lambda e, kt=kt, r=r: e.matmul(
                plog[:, :], lhsT=xT[:, kt, r * 128:(r + 1) * 128], rhs=rwb[:, kt, :], start=(kt == 0), stop=(kt == 7)))
        sq.step("vector", lambda e: e.tensor_tensor(out=lg[:, :], in0=plog[:, :], in1=rbrep[:, :], op=ALU.add))
        sq.step("vector", lambda e: e.tensor_reduce(out=mx[:, :], in_=lg[:, :], axis=AX.X, op=ALU.max))
        sq.step("vector", lambda e: e.tensor_scalar_mul(out=mx[:, :], in0=mx[:, :], scalar1=-1.0))
        sq.step("scalar", lambda e: e.activation(out=ex[:, :], in_=lg[:, :], func=AF.Exp, bias=mx[:, 0:1]))
        sq.step("vector", lambda e: e.tensor_reduce(out=sm[:, :], in_=ex[:, :], axis=AX.X, op=ALU.add))
        sq.step("vector", lambda e: e.reciprocal(out=rcs[:, :], in_=sm[:, :]))
        sq.step("vector", lambda e, r=r: e.tensor_scalar_mul(out=wgt[r][:, :], in0=ex[:, :], scalar1=rcs[:, 0:1]))

    for ei in range(8):
        for r in range(2):
            sq.step("vector", lambda e, r=r, ei=ei: e.tensor_scalar_mul(out=xsc[:, :], in0=lnf[r][:, :], scalar1=wgt[r][:, ei:ei + 1]))
            for kt in range(8):
                b.tp(xsc[:, kt * 128:(kt + 1) * 128], xscT[:, kt, r * 128:(r + 1) * 128])
        for nb in range(32):
            for kt in range(8):
                sq.dma("sync", wstg[:, 0:128], wi[ei, kt * 128:(kt + 1) * 128, nb * 128:(nb + 1) * 128])
                sq.step("vector", lambda e: e.tensor_copy(out=wstgb[:, 0:128], in_=wstg[:, 0:128]))
                sq.step("tensor", lambda e, kt=kt: e.matmul(
                    pi[:, :], lhsT=wstgb[:, 0:128], rhs=xscT[:, kt, :],
                    start=(kt == 0), stop=(kt == 7)))
            sq.step("scalar", lambda e, nb=nb: e.activation(out=iT[:, nb, :], in_=pi[:, :], func=AF.Relu))
        for kt in range(32):
            sq.dma("sync", wstg[:, 0:D0], wo[ei, kt * 128:(kt + 1) * 128, :])
            sq.step("vector", lambda e: e.tensor_copy(out=wstgb[:, 0:D0], in_=wstg[:, 0:D0]))
            for m in range(8):
                sq.step("tensor", lambda e, kt=kt, m=m, ei=ei: e.matmul(
                    po[:, m, :], lhsT=wstgb[:, m * 128:(m + 1) * 128], rhs=iT[:, kt, :],
                    start=(ei == 0 and kt == 0), stop=(ei == 7 and kt == 31), skip_group_check=True))
    for m in range(8):
        sq.step("vector", lambda e, m=m: e.tensor_copy(out=obf[:, :], in_=po[:, m, :]))
        for r in range(2):
            sq.step("tensor", lambda e, r=r: e.transpose(b.ptp[:, :], in_=obf[:, r * 128:(r + 1) * 128],
                                                         identity=b.ident[:, :]))
            sq.step("vector", lambda e, m=m, r=r: e.tensor_tensor(
                out=osb[r][:, m * 128:(m + 1) * 128], in0=b.ptp[:, :], in1=xs[r][:, m * 128:(m + 1) * 128],
                op=ALU.add))
    for r in range(2):
        sq.dma("gpsimd", out[r * 128:(r + 1) * 128, :], osb[r][:, :])
    return b.finish()
